# version 15
# baseline (speedup 1.0000x reference)
import numpy as np
import ml_dtypes
from contextlib import ExitStack
import concourse.bass as bass
import concourse.mybir as mybir
import concourse.dve_ops as dve_ops
import concourse.bass_utils as _bass_utils
from concourse.bass_utils import run_bass_kernel_spmd

_orig_run_command = _bass_utils.run_command
def _run_command_ldwopt(cmd, *a, **kw):
    if isinstance(cmd, list):
        cmd = ["--enable-ldw-opt=true" if c == "--enable-ldw-opt=false" else c
               for c in cmd]
    return _orig_run_command(cmd, *a, **kw)
_bass_utils.run_command = _run_command_ldwopt

F32 = mybir.dt.float32
BF16 = mybir.dt.bfloat16
AF = mybir.ActivationFunctionType
ALU = mybir.AluOpType
AX = mybir.AxisListType

B, T, D = 2, 4096, 2048
HQ, HKV, HD = 32, 8, 64
DH = 8192
NCORES = 8
TOK = 1024
TW = 512
NTT = 8
NDC = D // 128
NKC = (HKV * HD) // 128
NQC = D // 128
NHC = DH // 128
NKCH = T // 128
NPAIR = 16
EPS = 1e-5
VST = 520


def _q_perm():
    perm = []
    for j in range(16):
        p, i = j // 4, j % 4
        for h in (8 * p + i, 8 * p + 4 + i):
            perm.extend(range(64 * h, 64 * h + 64))
    return np.array(perm)


ENGS = ("tensor", "vector", "scalar", "sync", "gpsimd")
S_PE, S_VE, S_SC = "s_pe", "s_ve", "s_sc"


def is_dma_sem(s):
    return s.startswith("d")


class Em:
    def __init__(self):
        self.ops = {e: [] for e in ENGS}
        self.cnt = {}

    def emit(self, eng, fn, waits=(), inc=None):
        w = [(s, v) for t in waits if t is not None for (s, v) in [t] if v > 0]
        if eng == "vector":
            w.append((S_VE, self.cnt.get(S_VE, 0)))
            inc = S_VE
        self.ops[eng].append([fn, tuple(w), inc])
        if inc is not None:
            self.cnt[inc] = self.cnt.get(inc, 0) + 1
            return (inc, self.cnt[inc])
        return None

    def tok(self, sem):
        return (sem, self.cnt.get(sem, 0))

    def check_deadlock(self):
        pos = {e: 0 for e in ENGS}
        cnt = {}
        total = sum(len(v) for v in self.ops.values())
        done = 0
        progress = True
        while progress:
            progress = False
            for e in ENGS:
                while pos[e] < len(self.ops[e]):
                    fn, waits, inc = self.ops[e][pos[e]]
                    if any(cnt.get(s, 0) < v for (s, v) in waits):
                        break
                    if inc is not None:
                        cnt[inc] = cnt.get(inc, 0) + 1
                    pos[e] += 1
                    done += 1
                    progress = True
        if done != total:
            stuck = {e: (pos[e], len(self.ops[e]),
                         self.ops[e][pos[e]][1] if pos[e] < len(self.ops[e]) else None)
                     for e in ENGS}
            raise RuntimeError(f"DEADLOCK {done}/{total}; stuck={stuck}")


_NC_CACHE = {}


def build_nc(debug_names=()):
    key = tuple(sorted(debug_names))
    if key in _NC_CACHE:
        return _NC_CACHE[key]
    nc = bass.Bass("TRN2", target_bir_lowering=False)
    es = ExitStack()

    dr = {}
    def din(name, shape):
        dr[name] = nc.dram_tensor(name, list(shape), F32, kind="ExternalInput").ap()
    din("x_t", (NTT, 128, NDC * TW))
    din("cos_t", (NTT, 128, TW))
    din("sin_t", (NTT, 128, TW))
    din("wq_t", (NQC, 128, NDC * 128))
    din("wk_t", (NKC, 128, NDC * 128))
    din("wv_t", (128, NDC * TW))
    din("wg_t", (NHC, 128, NDC * 128))
    din("wu_t", (NHC, 128, NDC * 128))
    din("wd_t", (NQC, 128, NHC * 128))
    din("wo_t", (NQC, 128, NQC * 128))
    din("rot_t", (128, 128))
    din("bq_t", (128, NQC))
    din("bk_t", (128, NKC))
    din("bo_t", (128, NQC))
    din("bv_t", (1, 512))
    din("xr_t", (NQC, 128, TOK))
    out_t = nc.dram_tensor("out_t", [NQC, 128, TOK], F32, kind="ExternalOutput").ap()
    dbg = {}
    for name, shape in dict(
        d_kt=[128, NKC * T], d_qt=[128, NQC * TOK], d_v=[128, NKCH * VST],
        d_xn=[128, NDC * TW], d_negb=[128, 1], d_attn=[128, NPAIR * TOK],
        d_gated=[128, NHC * TW],
    ).items():
        if name in debug_names:
            dbg[name] = nc.dram_tensor(name, shape, F32, kind="ExternalOutput").ap()

    sb = lambda n, s, d: es.enter_context(nc.sbuf_tensor(n, list(s), d))
    muown = es.enter_context(nc.sbuf_tensor("muown", [1, TOK], F32, side="right"))
    rown = es.enter_context(nc.sbuf_tensor("rown", [1, TOK], F32, side="right"))
    ones_bf = sb("ones_bf", (128, 128), BF16)
    ones_f = sb("ones_f", (1, 128), F32)
    rot_sb = sb("rot_sb", (128, 128), BF16)
    bq_sb = sb("bq_sb", (128, NQC), F32)
    bk_sb = sb("bk_sb", (128, NKC), F32)
    bo_sb = sb("bo_sb", (128, NQC), F32)
    bv_sb = sb("bv_sb", (1, 512), BF16)
    negb_sb = sb("negb_sb", (128, 1), F32)
    nq_sb = sb("nq_sb", (1, 64), F32)
    nk_sb = sb("nk_sb", (1, 64), F32)
    sc1_sb = sb("sc1_sb", (1, 8), F32)

    pb01 = es.enter_context(nc.psum_tensor("pb01", [128, 2048], F32))
    pb2 = es.enter_context(nc.psum_tensor("pb2", [128, 1024], F32))
    pb3 = es.enter_context(nc.psum_tensor("pb3", [128, 1024], F32))

    em = Em()
    K = {}

    em.emit("vector", lambda v: v.memset(ones_bf[:, :], 1.0))
    em.emit("vector", lambda v: v.memset(ones_f[:, :], 1.0))
    em.emit("vector", lambda v: v.memset(nq_sb[:, :], 0.0))
    em.emit("vector", lambda v: v.memset(nk_sb[:, :], 0.0))
    em.emit("gpsimd", lambda g: g.dma_start(out=rot_sb[:, :], in_=dr["rot_t"][:, :]), inc="dcst")
    em.emit("gpsimd", lambda g: g.dma_start(out=bq_sb[:, :], in_=dr["bq_t"][:, :]), inc="dcst")
    em.emit("gpsimd", lambda g: g.dma_start(out=bk_sb[:, :], in_=dr["bk_t"][:, :]), inc="dcst")
    em.emit("gpsimd", lambda g: g.dma_start(out=bo_sb[:, :], in_=dr["bo_t"][:, :]), inc="dcst")
    em.emit("gpsimd", lambda g: g.dma_start(out=bv_sb[:, :], in_=dr["bv_t"][:, :]), inc="dcst")

    esK = ExitStack()
    kt_all = esK.enter_context(nc.sbuf_tensor("kt_all", [128, NKC * T], BF16))
    v_all = esK.enter_context(nc.sbuf_tensor("v_all", [128, NKCH * VST], BF16))
    qt_all = esK.enter_context(nc.sbuf_tensor("qt_all", [128, NQC * TOK], BF16))
    esA = ExitStack()
    sbA = lambda n, s, d: esA.enter_context(nc.sbuf_tensor(n, list(s), d))
    wk_sb = sbA("wk_sb", (128, 2 * NDC * 128), BF16)
    wv_sb = sbA("wv_sb", (128, NDC * TW), BF16)
    wq_sb = sbA("wq_sb", (128, 2 * NDC * 128), BF16)
    xt_sb = sbA("xt_sb", (128, 2 * NDC * TW), BF16)
    xf_sb = sbA("xf_sb", (128, 2 * TW), F32)
    cos_sb = sbA("cos_sb", (128, TW), F32)
    sin_sb = sbA("sin_sb", (128, TW), F32)
    scrf = sbA("scrf", (128, 2 * TW), F32)
    scrb = sbA("scrb", (128, 2 * TW), BF16)
    stat = sbA("stat", (1, 2 * TW), F32)

    for tc in range(NKCH):
        vv = v_all[:, VST * tc:VST * (tc + 1)].rearrange("p (g c) -> p g c", c=65)
        em.emit("vector", (lambda vv: lambda v: v.memset(vv[:, :, 64:65], 1.0))(vv))
    tok_wv = em.emit("gpsimd", lambda g: g.dma_start(out=wv_sb[:, :], in_=dr["wv_t"][:, :]), inc="dcst")
    tok_consts = tok_wv
    K["consts"] = tok_consts
    K["init_ve"] = em.tok(S_VE)

    pstat = pb01[:, 0:512]
    pnorm = pb01[:, 512:1024]
    pmub = pb01[:, 1024:1536]
    prb = pb01[:, 1536:2048]
    pp = [pb2[:, 0:512], pb2[:, 512:1024], pb3[:, 0:512], pb3[:, 512:1024]]

    state = dict(xt_free={}, rope_done=None, last_red=None,
                 wk_use={}, wq_use={}, pp_free=[None] * 4, scrb_war={},
                 prev_stats_consumed=None, nk_i=0, nq_i=0, xn_prev={})

    def A_tile(tt):
        own = tt < 2
        tpar = tt % 2
        xb = 8192 * tpar
        wx = [state["xt_free"].get(tpar), state.get("dbg_xn")]
        tok_xq = {}
        for qq in range(4):
            tok_xq[qq] = em.emit("gpsimd", (lambda tt, qq, xb: lambda g: g.dma_start(
                out=xt_sb[:, xb + 2048 * qq:xb + 2048 * (qq + 1)],
                in_=dr["x_t"][tt, :, 2048 * qq:2048 * (qq + 1)]))(tt, qq, xb),
                waits=(wx if qq == 0 else []), inc=f"dxt{tpar}{qq}")
        tok_x = tok_xq[3]
        tok_cos = em.emit("sync", (lambda tt: lambda s: s.dma_start(
            out=cos_sb[:, :], in_=dr["cos_t"][tt, :, :]))(tt),
            waits=[state["rope_done"]], inc="dcos")
        tok_sin = em.emit("sync", (lambda tt: lambda s: s.dma_start(
            out=sin_sb[:, :], in_=dr["sin_t"][tt, :, :]))(tt),
            waits=[state["rope_done"]], inc="dsin")

        for dc in range(NDC):
            xs = xt_sb[:, xb + TW * dc:xb + TW * (dc + 1)]
            w = [tok_xq[dc // 4]]
            if dc == 0:
                w += [state["prev_stats_consumed"], K["init_ve"]]
            em.emit("tensor", (lambda xs, dc: lambda t: t.matmul(
                pstat[0:1, :], ones_bf[:, 0:1], xs,
                start=(dc == 0), stop=(dc == NDC - 1)))(xs, dc), waits=w)
        prev_sq_mm = {}
        sqv = {}
        for dc in range(NDC):
            xs = xt_sb[:, xb + TW * dc:xb + TW * (dc + 1)]
            sp = dc % 2
            sqv[dc] = em.emit("scalar", (lambda xs, sp: lambda s: s.activation(
                scrb[:, TW * sp:TW * (sp + 1)], xs, AF.Square))(xs, sp),
                waits=[tok_xq[dc // 4], prev_sq_mm.get(dc - 2), state["scrb_war"].get(sp),
                       state.get("scrb_pe") if dc < 2 else None], inc=S_SC)
            prev_sq_mm[dc] = em.emit("tensor", (lambda dc, sp: lambda t: t.matmul(
                pstat[32:33, :], ones_bf[:, 0:1], scrb[:, TW * sp:TW * (sp + 1)],
                start=(dc == 0), stop=(dc == NDC - 1)))(dc, sp),
                waits=[sqv[dc]], inc=S_PE)
        stats_mm = prev_sq_mm[NDC - 1]

        mu_ap = muown[0:1, TW * tt:TW * (tt + 1)] if own else stat[0:1, 0:TW]
        wk_row = stat[0:1, TW:2 * TW]
        r_ap = rown[0:1, TW * tt:TW * (tt + 1)] if own else wk_row
        t1 = em.emit("vector", (lambda mu_ap: lambda v: v.tensor_scalar_mul(
            mu_ap, pstat[0:1, :], 1.0 / D))(mu_ap),
            waits=[stats_mm, state.get("prev_tmub")])
        t2 = em.emit("vector", (lambda mu_ap: lambda v: v.tensor_tensor(
            wk_row, mu_ap, mu_ap, ALU.mult))(mu_ap), waits=[state.get("prev_trb")])
        t3 = em.emit("vector", lambda v: v.scalar_tensor_tensor(
            wk_row, pstat[32:33, :], 1.0 / D, wk_row, ALU.mult, ALU.subtract))
        t4 = em.emit("vector", lambda v: v.tensor_scalar_add(wk_row, wk_row, EPS))
        t5 = em.emit("scalar", lambda s: s.activation(wk_row, wk_row, AF.Sqrt),
                     waits=[t4], inc=S_SC)
        tmub = em.emit("tensor", (lambda mu_ap: lambda t: t.matmul(
            pmub[:, :], ones_f[0:1, :], mu_ap, start=True, stop=True))(mu_ap),
            waits=[t1, em.tok(S_VE)])
        t6 = em.emit("vector", (lambda r_ap: lambda v: v.reciprocal(r_ap, wk_row))(r_ap),
                     waits=[t5])
        trb = em.emit("tensor", (lambda r_ap: lambda t: t.matmul(
            prb[:, :], ones_f[0:1, :], r_ap, start=True, stop=True))(r_ap),
            waits=[t6], inc=S_PE)
        state["prev_tmub"] = trb
        state["prev_trb"] = trb
        bcast = trb

        xn_tok = {}
        for dc in range(NDC):
            par = dc % 2
            xf = xf_sb[:, TW * par:TW * (par + 1)]
            w_xf = [xn_tok.get(dc - 2) or state["xn_prev"].get(NDC - 2 + dc)]
            tok_xf = em.emit("sync", (lambda tt, dc, xf: lambda s: s.dma_start(
                out=xf, in_=dr["x_t"][tt, :, TW * dc:TW * (dc + 1)]))(tt, dc, xf),
                waits=w_xf, inc=f"dxf{par}")
            ts_ = em.emit("vector", (lambda xf: lambda v: v.tensor_tensor(
                scrf[:, 0:TW], xf, pmub[:, :], ALU.subtract))(xf),
                waits=[tok_xf, bcast])
            xn_tok[dc] = em.emit("vector", (lambda dc, xb: lambda v: v.tensor_tensor(
                xt_sb[:, xb + TW * dc:xb + TW * (dc + 1)], scrf[:, 0:TW], prb[:, :],
                ALU.mult))(dc, xb), waits=[stats_mm])
        state["xn_prev"] = xn_tok
        state["prev_stats_consumed"] = t3
        if "d_xn" in dbg and tt == 0:
            state["dbg_xn"] = em.emit("gpsimd", lambda g: g.dma_start(
                out=dbg["d_xn"][:, :], in_=xt_sb[:, 0:NDC * TW]),
                waits=[xn_tok[NDC - 1]], inc="ddbg")

        def proj_oc(oc, wbuf, tok_w, b_sb, tgt_all, off, ncoll, nidx):
            par = oc % 2
            ppj = pp[par]
            prot = pp[2 + par]
            mtk = None
            for dc in range(NDC):
                w = [xn_tok[dc], tok_w]
                if dc == 0:
                    w += [state["pp_free"][par]]
                mtk = em.emit("tensor", (lambda ppj, wbuf, dc, xb: lambda t: t.matmul(
                    ppj[:, :], wbuf[:, 128 * dc:128 * (dc + 1)],
                    xt_sb[:, xb + TW * dc:xb + TW * (dc + 1)],
                    start=(dc == 0), stop=(dc == NDC - 1)))(ppj, wbuf, dc, xb),
                    waits=w, inc=(S_PE if dc == NDC - 1 else None))
            ev1 = em.emit("scalar", (lambda ppj, oc, b_sb: lambda s: s.activation(
                scrb[:, 0:TW], ppj[:, :], AF.Identity, bias=b_sb[:, oc:oc + 1]))(ppj, oc, b_sb),
                waits=[mtk, tok_consts, em.tok(S_VE), em.tok(S_SC)], inc=S_SC)
            state["scrb_war"][0] = ev1
            rtk = em.emit("tensor", (lambda prot, par: lambda t: t.matmul(
                prot[:, :], rot_sb[:, :], scrb[:, 0:TW], start=True, stop=True))(prot, par),
                waits=[ev1, tok_consts, state["pp_free"][2 + par]], inc=S_PE)
            r1 = em.emit("vector", lambda v: v.tensor_tensor(
                scrf[:, 0:TW], scrb[:, 0:TW], cos_sb[:, :], ALU.mult),
                waits=[tok_cos, ev1])
            r2 = em.emit("vector", (lambda prot: lambda v: v.tensor_tensor(
                scrf[:, TW:2 * TW], prot[:, :], sin_sb[:, :], ALU.mult))(prot),
                waits=[rtk, tok_sin])
            r3 = em.emit("vector", (lambda tgt_all, off: lambda v: v.tensor_tensor(
                tgt_all[:, off:off + TW], scrf[:, 0:TW], scrf[:, TW:2 * TW], ALU.add))(tgt_all, off))
            state["pp_free"][par] = r3
            state["pp_free"][2 + par] = r3
            sq = em.emit("vector", (lambda tgt_all, off: lambda v: v.tensor_tensor(
                scrb[:, 0:TW], tgt_all[:, off:off + TW], tgt_all[:, off:off + TW],
                ALU.mult))(tgt_all, off), waits=[rtk])
            state["scrb_war"][0] = sq
            nm = em.emit("tensor", lambda t: t.matmul(
                pnorm[0:1, :], ones_bf[:, 0:1], scrb[:, 0:TW], start=True, stop=True),
                waits=[sq, state["last_red"]], inc=S_PE)
            state["last_red"] = em.emit("vector", (lambda ncoll, nidx: lambda v: v.tensor_reduce(
                ncoll[0:1, nidx:nidx + 1], pnorm[0:1, :], AX.X, ALU.max))(ncoll, nidx),
                waits=[nm])
            return mtk

        for oc in range(NKC):
            wpar = (tt * NKC + oc) % 2
            wslot = wk_sb[:, 2048 * wpar:2048 * (wpar + 1)]
            tok_w = em.emit("gpsimd", (lambda oc, wslot: lambda g: g.dma_start(
                out=wslot, in_=dr["wk_t"][oc, :, :]))(oc, wslot),
                waits=[state["wk_use"].get(wpar)], inc=f"dwk{wpar}")
            state["wk_use"][wpar] = proj_oc(oc, wslot, tok_w, bk_sb, kt_all,
                                            4096 * oc + TW * tt, nk_sb, state["nk_i"])
            state["nk_i"] += 1

        for sub in range(4):
            par = sub % 2
            ppj = pp[par]
            mtk = None
            for dc in range(NDC):
                w = [xn_tok[dc], tok_wv]
                if dc == 0:
                    w += [state["pp_free"][par]]
                mtk = em.emit("tensor", (lambda ppj, dc, sub, xb: lambda t: t.matmul(
                    ppj[:, :], xt_sb[:, xb + TW * dc + 128 * sub: xb + TW * dc + 128 * sub + 128],
                    wv_sb[:, TW * dc:TW * (dc + 1)],
                    start=(dc == 0), stop=False))(ppj, dc, sub, xb), waits=w)
            mtk = em.emit("tensor", (lambda ppj: lambda t: t.matmul(
                ppj[:, :], ones_bf[0:1, :], bv_sb[0:1, :], start=False, stop=True))(ppj),
                waits=[tok_consts], inc=S_PE)
            tc = tt * 4 + sub
            ev = None
            for g in range(8):
                ev = em.emit("scalar", (lambda ppj, tc, g: lambda s: s.copy(
                    v_all[:, VST * tc + 65 * g: VST * tc + 65 * g + 64],
                    ppj[:, 64 * g:64 * g + 64]))(ppj, tc, g),
                    waits=[mtk] + ([em.tok(S_VE)] if g == 0 else []),
                    inc=(S_SC if g == 7 else None))
            state["pp_free"][par] = ev

        if own:
            for oc in range(NQC):
                wpar = oc % 2
                wslot = wq_sb[:, 2048 * wpar:2048 * (wpar + 1)]
                tok_w = em.emit("gpsimd", (lambda oc, wslot: lambda g: g.dma_start(
                    out=wslot, in_=dr["wq_t"][oc, :, :]))(oc, wslot),
                    waits=[state["wq_use"].get(wpar)], inc=f"dwq{wpar}")
                state["wq_use"][wpar] = proj_oc(oc, wslot, tok_w, bq_sb, qt_all,
                                                TOK * oc + TW * tt, nq_sb, state["nq_i"])
                state["nq_i"] += 1

        state["xt_free"][tpar] = em.tok(S_PE)
        state["scrb_pe"] = em.tok(S_PE)
        state["rope_done"] = em.tok(S_VE)

    for tt in range(NTT):
        A_tile(tt)

    a1 = em.emit("vector", lambda v: v.tensor_reduce(
        sc1_sb[0:1, 0:1], nq_sb[0:1, 0:32], AX.X, ALU.max), waits=[state["last_red"]])
    a2 = em.emit("vector", lambda v: v.tensor_reduce(
        sc1_sb[0:1, 1:2], nk_sb[0:1, 0:32], AX.X, ALU.max))
    a3 = em.emit("vector", lambda v: v.tensor_tensor(
        sc1_sb[0:1, 2:3], sc1_sb[0:1, 0:1], sc1_sb[0:1, 1:2], ALU.mult))
    a4 = em.emit("scalar", lambda s: s.activation(
        sc1_sb[0:1, 3:4], sc1_sb[0:1, 2:3], AF.Sqrt), waits=[a3], inc=S_SC)
    a5 = em.emit("vector", lambda v: v.tensor_scalar_mul(
        sc1_sb[0:1, 4:5], sc1_sb[0:1, 3:4], -0.125), waits=[a4])
    a6 = em.emit("tensor", lambda t: t.matmul(
        pb01[:, 0:1], ones_f[0:1, :], sc1_sb[0:1, 4:5], start=True, stop=True),
        waits=[a5, em.tok(S_VE)], inc=S_PE)
    a7 = em.emit("vector", lambda v: v.tensor_copy(negb_sb[:, :], pb01[:, 0:1]),
                 waits=[a6])
    K["negb"] = a7
    for nm_, buf in (("d_negb", negb_sb), ("d_kt", kt_all), ("d_qt", qt_all), ("d_v", v_all)):
        if nm_ in dbg:
            em.emit("gpsimd", (lambda nm_, buf: lambda g: g.dma_start(
                out=dbg[nm_][:, :], in_=buf[:, :]))(nm_, buf),
                waits=[em.tok(S_VE), em.tok(S_SC)], inc="ddbg")
    esA.close()

    esB = ExitStack()
    attn_sb = es.enter_context(nc.sbuf_tensor("attn_sb", [128, NPAIR * TOK], BF16, side="right"))
    NSLOT = 7
    DLY = 6
    ptab = esB.enter_context(nc.sbuf_tensor("ptab", [128, NSLOT * 2 * TOK], BF16))
    rz_sb = esB.enter_context(nc.sbuf_tensor("rz_sb", [1, 2 * TOK], F32))
    rzs_sb = esB.enter_context(nc.sbuf_tensor("rzs_sb", [1, TOK], F32))
    atmp = esB.enter_context(nc.sbuf_tensor("atmp", [128, TOK], BF16))

    pscore = pb01
    pava, pavb = pb2, pb3
    negb_tok = K["negb"]
    a_ready = em.tok(S_VE)
    a_ready_sc = em.tok(S_SC)
    prev_evac = None
    av_final = None

    for j in range(NPAIR):
        p = j // 4
        m1 = em.emit("vector", lambda v: v.memset(pavb[0:1, :], 0.0),
                     waits=[av_final, prev_evac])
        em.emit("vector", lambda v: v.memset(pavb[64:128, :], 0.0))
        bmemset = em.emit("vector", lambda v: v.memset(pava[0:65, :], 0.0))
        exp_ab = {}
        s_ab = {}
        if j == 0:
            prev_pair_exp = {}
            av_b = {}

        def S_half(c, h):
            w_s = [a_ready, a_ready_sc, negb_tok, exp_ab.get((c - 1, h))]
            if j > 0 and c == 0:
                w_s.append(prev_pair_exp.get(h))
            last = None
            for hh in range(2):
                base = 64 * hh
                off = 1024 * h + 512 * hh
                last = em.emit("tensor", (lambda p, j, c, h, base, off: lambda t: t.matmul(
                    pscore[:, off:off + 512],
                    kt_all[base:base + 64, 4096 * p + 128 * c:4096 * p + 128 * (c + 1)],
                    qt_all[base:base + 64, TOK * j + 512 * h:TOK * j + 512 * (h + 1)],
                    start=True, stop=True))(p, j, c, h, base, off),
                    waits=(w_s if hh == 0 else []),
                    inc=(S_PE if hh == 1 else None))
            s_ab[(c, h)] = last

        def EXP(c, h):
            g = j * NKCH + c
            par = g % NSLOT
            w_e = [s_ab[(c, h)], av_b.get((g - NSLOT, h))]
            exp_ab[(c, h)] = em.emit("scalar", (lambda par, h: lambda s: s.activation(
                ptab[:, 2048 * par + 1024 * h:2048 * par + 1024 * (h + 1)],
                pscore[:, 1024 * h:1024 * (h + 1)], AF.Exp,
                bias=negb_sb[:, :], scale=0.125))(par, h), waits=w_e, inc=S_SC)

        def AV(c, h):
            g = j * NKCH + c
            par = g % NSLOT
            first = (c == 0)
            pt_a = ptab[:, 2048 * par + 1024 * h:2048 * par + 1024 * h + 512]
            pt_b = ptab[:, 2048 * par + 1024 * h + 512:2048 * par + 1024 * (h + 1)]
            w_av = [exp_ab[(c, h)]]
            if first and h == 0:
                w_av += [prev_evac, bmemset]
            em.emit("tensor", (lambda c, p, h, pt_a: lambda t: t.matmul(
                pava[0:64, 512 * h:512 * (h + 1)],
                v_all[:, VST * c + 65 * (2 * p):VST * c + 65 * (2 * p) + 64],
                pt_a, start=False, stop=False, skip_group_check=True,
                tile_position=(0, 0)))(c, p, h, pt_a), waits=w_av)
            em.emit("tensor", (lambda c, p, h, pt_b: lambda t: t.matmul(
                pavb[64:128, 512 * h:512 * (h + 1)],
                v_all[:, VST * c + 65 * (2 * p + 1):VST * c + 65 * (2 * p + 1) + 64],
                pt_b, start=False, stop=False, skip_group_check=True,
                tile_position=(0, 64)))(c, p, h, pt_b))
            em.emit("tensor", (lambda c, h, pt_a: lambda t: t.matmul(
                pava[64:65, 512 * h:512 * (h + 1)], ones_bf[:, 0:1],
                pt_a, start=False, stop=False, skip_group_check=True,
                tile_position=(0, 64)))(c, h, pt_a))
            av_b[(g, h)] = em.emit("tensor", (lambda c, h, pt_b: lambda t: t.matmul(
                pavb[0:1, 512 * h:512 * (h + 1)], ones_bf[:, 0:1],
                pt_b, start=False, stop=False, skip_group_check=True,
                tile_position=(0, 0)))(c, h, pt_b), inc=S_PE)

        for c in range(NKCH):
            for h in range(2):
                S_half(c, h)
                EXP(c, h)
                if c >= DLY:
                    AV(c - DLY, h)
        for c in range(NKCH - DLY, NKCH):
            AV(c, 0)
            AV(c, 1)
        av_final = av_b[(j * NKCH + NKCH - 1, 1)]
        prev_pair_exp = {h: exp_ab[(NKCH - 1, h)] for h in range(2)}
        q1 = em.emit("vector", lambda v: v.reciprocal(rz_sb[0:1, 0:TOK], pava[64:65, :]),
                     waits=[av_final])
        q2 = em.emit("vector", lambda v: v.reciprocal(rz_sb[0:1, TOK:2 * TOK], pavb[0:1, :]))
        q3 = em.emit("vector", lambda v: v.tensor_copy(atmp[0:64, :], pava[0:64, :]))
        q4 = em.emit("vector", lambda v: v.tensor_copy(atmp[64:128, :], pavb[64:128, :]))
        bt = None
        for h in range(2):
            em.emit("tensor", (lambda h: lambda t: t.matmul(
                pava[0:64, 512 * h:512 * (h + 1)], ones_f[0:1, 0:64],
                rz_sb[0:1, 512 * h:512 * (h + 1)], start=True, stop=True,
                skip_group_check=True))(h), waits=[q2, q3])
            bt = em.emit("tensor", (lambda h: lambda t: t.matmul(
                pavb[64:128, 512 * h:512 * (h + 1)], ones_f[0:1, 0:64],
                rz_sb[0:1, TOK + 512 * h:TOK + 512 * (h + 1)], start=True, stop=True,
                skip_group_check=True))(h),
                waits=[q4], inc=(S_PE if h == 1 else None))
        em.emit("vector", (lambda j: lambda v: v.tensor_tensor(
            attn_sb[0:64, TOK * j:TOK * (j + 1)], atmp[0:64, :], pava[0:64, :],
            ALU.mult))(j), waits=[bt])
        prev_evac = em.emit("vector", (lambda j: lambda v: v.tensor_tensor(
            attn_sb[64:128, TOK * j:TOK * (j + 1)], atmp[64:128, :], pavb[64:128, :],
            ALU.mult))(j))
    K["attn_done"] = prev_evac
    if "d_attn" in dbg:
        em.emit("gpsimd", lambda g: g.dma_start(out=dbg["d_attn"][:, :], in_=attn_sb[:, :]),
                waits=[prev_evac], inc="ddbg")
    esB.close()
    esK.close()

    esC = ExitStack()
    sbC = lambda n, s, d: esC.enter_context(nc.sbuf_tensor(n, list(s), d))
    gated = sbC("gated", (128, NHC * TW), BF16)
    xt2 = sbC("xt2", (128, NDC * TW), BF16)
    xf2 = sbC("xf2", (128, 2 * TW), F32)
    scr2 = sbC("scr2", (128, TW), F32)
    silu_sb = sbC("silu_sb", (128, 2 * TW), BF16)
    wg_sb = sbC("wg_sb", (128, 2 * NDC * 128), BF16)
    wu_sb = sbC("wu_sb", (128, 2 * NDC * 128), BF16)
    wd_sb = sbC("wd_sb", (128, 2 * NHC * 128), BF16)
    wo_sb = sbC("wo_sb", (128, 2 * NQC * 128), BF16)
    xr_sb = sbC("xr_sb", (128, 2 * TW), F32)
    out_sb = sbC("out_sb", (128, 2 * TW), F32)

    pmub2 = pb01[:, 0:512]
    prb2 = pb01[:, 512:1024]
    pg = [pb01[:, 1024:1536], pb01[:, 1536:2048]]
    pu = [pb2[:, 0:512], pb2[:, 512:1024]]
    pres = [pb3[:, 0:512], pb3[:, 512:1024]]
    attn_done = K["attn_done"]
    dbg_tok = em.tok("ddbg") if dbg else None
    cd = dict(wg_last={}, wu_last={}, wd_last={}, wo_last={}, gate_free={},
              silu_free={}, ev_last={}, st_last={}, xr_last={}, n_out=0,
              xn2_carry=None)

    for th in range(2):
        w0 = ([attn_done, dbg_tok] if th == 0
              else [em.tok(S_PE), em.tok(S_VE), em.tok("ddbg") if dbg else None])
        tok_x2 = em.emit("gpsimd", (lambda th: lambda g: g.dma_start(
            out=xt2[:, :], in_=dr["x_t"][th, :, :]))(th), waits=w0, inc="dxt2")
        bc1 = em.emit("tensor", (lambda th: lambda t: t.matmul(
            pmub2[:, :], ones_f[0:1, :], muown[0:1, TW * th:TW * (th + 1)],
            start=True, stop=True))(th), waits=[attn_done] + (w0 if th else []))
        bc2 = em.emit("tensor", (lambda th: lambda t: t.matmul(
            prb2[:, :], ones_f[0:1, :], rown[0:1, TW * th:TW * (th + 1)],
            start=True, stop=True))(th), inc=S_PE)
        xn2_tok = {}
        for dc in range(NDC):
            par = dc % 2
            xf = xf2[:, TW * par:TW * (par + 1)]
            w = [xn2_tok.get(dc - 2) or cd["xn2_carry"] or attn_done, dbg_tok]
            tok_xf = em.emit("sync", (lambda th, dc, xf: lambda s: s.dma_start(
                out=xf, in_=dr["x_t"][th, :, TW * dc:TW * (dc + 1)]))(th, dc, xf),
                waits=w, inc=f"dxf2{par}")
            em.emit("vector", (lambda xf: lambda v: v.tensor_tensor(
                scr2[:, :], xf, pmub2[:, :], ALU.subtract))(xf),
                waits=[tok_xf, bc2])
            xn2_tok[dc] = em.emit("vector", (lambda dc: lambda v: v.tensor_tensor(
                xt2[:, TW * dc:TW * (dc + 1)], scr2[:, :], prb2[:, :], ALU.mult))(dc),
                waits=[tok_x2])
        cd["xn2_carry"] = xn2_tok[NDC - 1]

        gate_ev = {}
        for hc in range(NHC):
            par = hc % 2
            tok_wg = em.emit("gpsimd", (lambda hc, par: lambda g: g.dma_start(
                out=wg_sb[:, 2048 * par:2048 * (par + 1)], in_=dr["wg_t"][hc, :, :]))(hc, par),
                waits=[cd["wg_last"].get(par)], inc=f"dwg{par}")
            tok_wu = em.emit("gpsimd", (lambda hc, par: lambda g: g.dma_start(
                out=wu_sb[:, 2048 * par:2048 * (par + 1)], in_=dr["wu_t"][hc, :, :]))(hc, par),
                waits=[cd["wu_last"].get(par)], inc=f"dwu{par}")
            mg = mu_ = None
            for dc in range(NDC):
                w = [xn2_tok[dc], tok_wg]
                if dc == 0:
                    w.append(cd["silu_free"].get(par))
                mg = em.emit("tensor", (lambda par, dc: lambda t: t.matmul(
                    pg[par][:, :], wg_sb[:, 2048 * par + 128 * dc:2048 * par + 128 * (dc + 1)],
                    xt2[:, TW * dc:TW * (dc + 1)],
                    start=(dc == 0), stop=(dc == NDC - 1)))(par, dc),
                    waits=w, inc=(S_PE if dc == NDC - 1 else None))
            cd["wg_last"][par] = mg
            for dc in range(NDC):
                w = [xn2_tok[dc], tok_wu]
                if dc == 0:
                    w.append(cd["gate_free"].get(par))
                mu_ = em.emit("tensor", (lambda par, dc: lambda t: t.matmul(
                    pu[par][:, :], wu_sb[:, 2048 * par + 128 * dc:2048 * par + 128 * (dc + 1)],
                    xt2[:, TW * dc:TW * (dc + 1)],
                    start=(dc == 0), stop=(dc == NDC - 1)))(par, dc),
                    waits=w, inc=(S_PE if dc == NDC - 1 else None))
            cd["wu_last"][par] = mu_
            sl = em.emit("scalar", (lambda par: lambda s: s.activation(
                silu_sb[:, TW * par:TW * (par + 1)], pg[par][:, :], AF.Silu))(par),
                waits=[mg, cd["gate_free"].get(par)], inc=S_SC)
            gate_ev[hc] = em.emit("vector", (lambda hc, par: lambda v: v.tensor_tensor(
                gated[:, TW * hc:TW * (hc + 1)], silu_sb[:, TW * par:TW * (par + 1)],
                pu[par][:, :], ALU.mult))(hc, par), waits=[sl, mu_])
            cd["gate_free"][par] = gate_ev[hc]
            cd["silu_free"][par] = gate_ev[hc]

        for oc in range(NQC):
            par = oc % 2
            tok_wd = em.emit("gpsimd", (lambda oc, par: lambda g: g.dma_start(
                out=wd_sb[:, 8192 * par:8192 * (par + 1)], in_=dr["wd_t"][oc, :, :]))(oc, par),
                waits=[cd["wd_last"].get(par)], inc=f"dwd{par}")
            tok_wo = em.emit("gpsimd", (lambda oc, par: lambda g: g.dma_start(
                out=wo_sb[:, 2048 * par:2048 * (par + 1)], in_=dr["wo_t"][oc, :, :]))(oc, par),
                waits=[cd["wo_last"].get(par)], inc=f"dwo{par}")
            tok_xr = em.emit("sync", (lambda oc, par, th: lambda s: s.dma_start(
                out=xr_sb[:, TW * par:TW * (par + 1)],
                in_=dr["xr_t"][oc, :, TW * th:TW * (th + 1)]))(oc, par, th),
                waits=[cd["xr_last"].get(par)], inc=f"dxr{par}")
            mm = None
            for cc in range(NQC):
                w = [attn_done, tok_wo]
                if cc == 0:
                    w.append(cd["ev_last"].get(par))
                mm = em.emit("tensor", (lambda par, cc, th: lambda t: t.matmul(
                    pres[par][:, :], wo_sb[:, 2048 * par + 128 * cc:2048 * par + 128 * (cc + 1)],
                    attn_sb[:, TOK * cc + TW * th:TOK * cc + TW * (th + 1)],
                    start=(cc == 0), stop=False))(par, cc, th), waits=w)
            for hc in range(NHC):
                mm = em.emit("tensor", (lambda par, hc: lambda t: t.matmul(
                    pres[par][:, :], wd_sb[:, 8192 * par + 128 * hc:8192 * par + 128 * (hc + 1)],
                    gated[:, TW * hc:TW * (hc + 1)],
                    start=False, stop=(hc == NHC - 1)))(par, hc),
                    waits=[gate_ev[hc], tok_wd], inc=(S_PE if hc == NHC - 1 else None))
            cd["wd_last"][par] = mm
            cd["wo_last"][par] = mm
            ev = em.emit("vector", (lambda par, oc: lambda v: v.scalar_tensor_tensor(
                out_sb[:, TW * par:TW * (par + 1)], pres[par][:, :], bo_sb[:, oc:oc + 1],
                xr_sb[:, TW * par:TW * (par + 1)], ALU.add, ALU.add))(par, oc),
                waits=[mm, tok_xr, cd["st_last"].get(par), K["consts"]])
            cd["ev_last"][par] = ev
            cd["xr_last"][par] = ev
            st = em.emit("sync", (lambda par, oc, th: lambda s: s.dma_start(
                out=out_t[oc, :, TW * th:TW * (th + 1)],
                in_=out_sb[:, TW * par:TW * (par + 1)]))(par, oc, th),
                waits=[ev], inc=f"dout{par}")
            cd["st_last"][par] = st
            cd["n_out"] += 1
        if "d_gated" in dbg and th == 0:
            em.emit("gpsimd", lambda g: g.dma_start(out=dbg["d_gated"][:, :], in_=gated[:, :]),
                    waits=[gate_ev[NHC - 1]], inc="ddbg")

    final_waits = [em.tok("dout0"), em.tok("dout1")]
    if dbg:
        final_waits.append(em.tok("ddbg"))
    em.emit("sync", lambda s: None, waits=final_waits)
    esC.close()

    em.check_deadlock()
    sem_es = ExitStack()
    sems = {n: sem_es.enter_context(nc.semaphore(n)) for n in sorted(em.cnt)}
    blk = sem_es.enter_context(nc.Block())

    def emit_engine(eng_name):
        ops = em.ops[eng_name]
        def body(eng):
            waited = {}
            for fn, waits, inc in ops:
                for (s, v) in waits:
                    v16 = 16 * v if is_dma_sem(s) else v
                    if waited.get(s, -1) >= v16:
                        continue
                    eng.wait_ge(sems[s], v16)
                    waited[s] = v16
                inst = fn(eng)
                if inc is not None:
                    if inst is None:
                        raise RuntimeError(f"inc op returned None on {eng_name}")
                    inst.then_inc(sems[inc], 16 if is_dma_sem(inc) else 1)
        return body

    blk.tensor(emit_engine("tensor"))
    blk.vector(emit_engine("vector"))
    blk.scalar(emit_engine("scalar"))
    blk.sync(emit_engine("sync"))
    blk.gpsimd(emit_engine("gpsimd"))
    sem_es.close()
    es.close()
    _NC_CACHE[key] = nc
    return nc


def _prep_inputs(inp):
    f32 = np.float32
    x = np.ascontiguousarray(np.asarray(inp["L0_attn_x"], f32))
    perm = _q_perm()
    Wq = np.ascontiguousarray(np.asarray(inp["L0_attn_Wq"], f32)[perm, :])
    bq = np.ascontiguousarray(np.asarray(inp["L0_attn_bq"], f32)[perm])
    Wk = np.asarray(inp["L0_attn_Wk"], f32)
    bk = np.asarray(inp["L0_attn_bk"], f32)
    Wv = np.asarray(inp["L0_attn_Wv"], f32)
    bv = np.asarray(inp["L0_attn_bv"], f32)
    Wo = np.ascontiguousarray(np.asarray(inp["L0_attn_Wo"], f32)[:, perm])
    bo = np.asarray(inp["L0_attn_bo"], f32)
    Wg = np.asarray(inp["L0_ffn_w_gate"], f32)
    Wu = np.asarray(inp["L0_ffn_w_up"], f32)
    Wd = np.asarray(inp["L0_ffn_w_down"], f32)

    def blockT(W, noc):
        O, Kd = W.shape
        nkc = Kd // 128
        a = W.reshape(noc, 128, nkc, 128).transpose(0, 3, 2, 1)
        return np.ascontiguousarray(a.reshape(noc, 128, nkc * 128))

    wq_t = blockT(Wq, NQC)
    wk_t = blockT(Wk, NKC)
    wg_t = blockT(Wg, NHC)
    wu_t = blockT(Wu, NHC)
    wd_t = blockT(Wd, NQC)
    wo_t = blockT(Wo, NQC)
    wv_t = np.ascontiguousarray(Wv.reshape(512, NDC, 128).transpose(2, 1, 0).reshape(128, NDC * 512))

    rot = np.zeros((128, 128), f32)
    for m in range(128):
        rot[64 * (m // 64) + ((m % 64) + 32) % 64, m] = 1.0
    inv = 1.0 / (10000.0 ** (np.arange(0, HD, 2, dtype=np.float64) / HD))
    ang = np.arange(T, dtype=np.float64)[:, None] * inv[None, :]
    cosf = np.cos(ang).astype(f32)
    sinf = np.sin(ang).astype(f32)
    prow = np.arange(128)
    irow = (prow % 64) % 32
    sgn = np.where((prow % 64) < 32, -1.0, 1.0).astype(f32)
    cos_full = np.ascontiguousarray(cosf[:, irow].T)
    sin_full = np.ascontiguousarray(sinf[:, irow].T * sgn[:, None])

    bq_t = np.ascontiguousarray(bq.reshape(NQC, 128).T)
    bk_t = np.ascontiguousarray(bk.reshape(NKC, 128).T)
    bo_t = np.ascontiguousarray(bo.reshape(NQC, 128).T)
    bv_t = np.ascontiguousarray(bv.reshape(1, 512))

    shared = dict(wq_t=wq_t, wk_t=wk_t, wv_t=wv_t, wg_t=wg_t, wu_t=wu_t,
                  wd_t=wd_t, wo_t=wo_t, rot_t=rot, bq_t=bq_t, bk_t=bk_t,
                  bo_t=bo_t, bv_t=bv_t)

    in_maps = []
    for c in range(NCORES):
        b, s = c // 4, c % 4
        order = [2 * s, 2 * s + 1] + [i for i in range(NTT) if i not in (2 * s, 2 * s + 1)]
        xT = x[b].T
        xr4 = xT.reshape(NDC, 128, NTT, TW)
        x_t = np.ascontiguousarray(
            xr4.transpose(2, 1, 0, 3)[order].reshape(NTT, 128, NDC * TW))
        cos_t = np.ascontiguousarray(cos_full.reshape(128, NTT, TW).transpose(1, 0, 2)[order])
        sin_t = np.ascontiguousarray(sin_full.reshape(128, NTT, TW).transpose(1, 0, 2)[order])
        xr_t = np.ascontiguousarray(xT[:, TOK * s:TOK * (s + 1)].reshape(NDC, 128, TOK))
        m = dict(shared)
        m.update(x_t=x_t, cos_t=cos_t, sin_t=sin_t, xr_t=xr_t)
        in_maps.append(m)
    return in_maps


def kernel(**inputs):
    in_maps = _prep_inputs(inputs)
    nc = build_nc()
    res = run_bass_kernel_spmd(nc, in_maps, core_ids=list(range(NCORES)))
    out = np.empty((B, T, D), np.float32)
    for c in range(NCORES):
        b, s = c // 4, c % 4
        o = np.asarray(res.results[c]["out_t"])
        out[b, TOK * s:TOK * (s + 1), :] = o.transpose(2, 0, 1).reshape(TOK, D)
    return out


# version 17
# speedup vs baseline: 1.0118x; 1.0118x over previous
import numpy as np
import ml_dtypes
from contextlib import ExitStack
import concourse.bass as bass
import concourse.mybir as mybir
import concourse.bass_utils as _bass_utils
from concourse.bass_utils import run_bass_kernel_spmd

_orig_run_command = _bass_utils.run_command
def _run_command_ldwopt(cmd, *a, **kw):
    if isinstance(cmd, list):
        cmd = ["--enable-ldw-opt=true" if c == "--enable-ldw-opt=false" else c
               for c in cmd]
    return _orig_run_command(cmd, *a, **kw)
_bass_utils.run_command = _run_command_ldwopt

F32 = mybir.dt.float32
BF16 = mybir.dt.bfloat16
AF = mybir.ActivationFunctionType
ALU = mybir.AluOpType
AX = mybir.AxisListType

B, T, D = 2, 4096, 2048
HQ, HKV, HD = 32, 8, 64
DH = 8192
NCORES = 8
TOK = 1024
TW = 512
NTT = 8
NDC = D // 128
NKC = (HKV * HD) // 128
NQC = D // 128
NHC = DH // 128
NKCH = T // 128
NPAIR = 16
EPS = 1e-5
VST = 520


def _q_perm():
    perm = []
    for j in range(16):
        p, i = j // 4, j % 4
        for h in (8 * p + i, 8 * p + 4 + i):
            perm.extend(range(64 * h, 64 * h + 64))
    return np.array(perm)


ENGS = ("tensor", "vector", "scalar", "sync", "gpsimd")
S_PE, S_VE, S_SC = "s_pe", "s_ve", "s_sc"


def is_dma_sem(s):
    return s.startswith("d")


class Em:
    def __init__(self):
        self.ops = {e: [] for e in ENGS}
        self.cnt = {}

    def emit(self, eng, fn, waits=(), inc=None):
        w = [(s, v) for t in waits if t is not None for (s, v) in [t] if v > 0]
        if eng == "vector":
            w.append((S_VE, self.cnt.get(S_VE, 0)))
            inc = S_VE
        self.ops[eng].append([fn, tuple(w), inc])
        if inc is not None:
            self.cnt[inc] = self.cnt.get(inc, 0) + 1
            return (inc, self.cnt[inc])
        return None

    def tok(self, sem):
        return (sem, self.cnt.get(sem, 0))

    def check_deadlock(self):
        pos = {e: 0 for e in ENGS}
        cnt = {}
        total = sum(len(v) for v in self.ops.values())
        done = 0
        progress = True
        while progress:
            progress = False
            for e in ENGS:
                while pos[e] < len(self.ops[e]):
                    fn, waits, inc = self.ops[e][pos[e]]
                    if any(cnt.get(s, 0) < v for (s, v) in waits):
                        break
                    if inc is not None:
                        cnt[inc] = cnt.get(inc, 0) + 1
                    pos[e] += 1
                    done += 1
                    progress = True
        if done != total:
            stuck = {e: (pos[e], len(self.ops[e]),
                         self.ops[e][pos[e]][1] if pos[e] < len(self.ops[e]) else None)
                     for e in ENGS}
            raise RuntimeError(f"DEADLOCK {done}/{total}; stuck={stuck}")


_NC_CACHE = {}


def build_nc(debug_names=()):
    key = tuple(sorted(debug_names))
    if key in _NC_CACHE:
        return _NC_CACHE[key]
    nc = bass.Bass("TRN2", target_bir_lowering=False)
    es = ExitStack()

    dr = {}
    def din(name, shape):
        dr[name] = nc.dram_tensor(name, list(shape), F32, kind="ExternalInput").ap()
    din("x_t", (NTT, 128, NDC * TW))
    din("cos_t", (NTT, 128, TW))
    din("sin_t", (NTT, 128, TW))
    din("wq_t", (NQC, 128, NDC * 128))
    din("wk_t", (NKC, 128, NDC * 128))
    din("wv_t", (128, NDC * TW))
    din("wg_t", (NHC, 128, NDC * 128))
    din("wu_t", (NHC, 128, NDC * 128))
    din("wd_t", (NQC, 128, NHC * 128))
    din("wo_t", (NQC, 128, NQC * 128))
    din("rot_t", (128, 128))
    din("bq_t", (128, NQC))
    din("bk_t", (128, NKC))
    din("bo_t", (128, NQC))
    din("bv_t", (1, 512))
    din("xr_t", (NQC, 128, TOK))
    out_t = nc.dram_tensor("out_t", [NQC, 128, TOK], F32, kind="ExternalOutput").ap()
    dbg = {}
    for name, shape in dict(
        d_kt=[128, NKC * T], d_qt=[128, NQC * TOK], d_v=[128, NKCH * VST],
        d_xn=[128, NDC * TW], d_negb=[128, 1], d_attn=[128, NPAIR * TOK],
        d_gated=[128, NHC * TW],
    ).items():
        if name in debug_names:
            dbg[name] = nc.dram_tensor(name, shape, F32, kind="ExternalOutput").ap()

    sb = lambda n, s, d: es.enter_context(nc.sbuf_tensor(n, list(s), d))
    muown = es.enter_context(nc.sbuf_tensor("muown", [1, TOK], F32, side="right"))
    rown = es.enter_context(nc.sbuf_tensor("rown", [1, TOK], F32, side="right"))
    ones_bf = sb("ones_bf", (128, 128), BF16)
    ones_f = sb("ones_f", (1, 128), F32)
    rot_sb = sb("rot_sb", (128, 128), BF16)
    bq_sb = sb("bq_sb", (128, NQC), F32)
    bk_sb = sb("bk_sb", (128, NKC), F32)
    bo_sb = sb("bo_sb", (128, NQC), F32)
    bv_sb = sb("bv_sb", (1, 512), BF16)
    negb_sb = sb("negb_sb", (128, 1), F32)
    nq_sb = sb("nq_sb", (1, 64), F32)
    nk_sb = sb("nk_sb", (1, 64), F32)
    sc1_sb = sb("sc1_sb", (1, 8), F32)

    pb01 = es.enter_context(nc.psum_tensor("pb01", [128, 2048], F32))
    pb2 = es.enter_context(nc.psum_tensor("pb2", [128, 1024], F32))
    pb3 = es.enter_context(nc.psum_tensor("pb3", [128, 1024], F32))

    em = Em()
    K = {}

    em.emit("vector", lambda v: v.memset(ones_bf[:, :], 1.0))
    em.emit("vector", lambda v: v.memset(ones_f[:, :], 1.0))
    em.emit("vector", lambda v: v.memset(nq_sb[:, :], 0.0))
    em.emit("vector", lambda v: v.memset(nk_sb[:, :], 0.0))
    em.emit("gpsimd", lambda g: g.dma_start(out=rot_sb[:, :], in_=dr["rot_t"][:, :]), inc="dcst")
    em.emit("gpsimd", lambda g: g.dma_start(out=bq_sb[:, :], in_=dr["bq_t"][:, :]), inc="dcst")
    em.emit("gpsimd", lambda g: g.dma_start(out=bk_sb[:, :], in_=dr["bk_t"][:, :]), inc="dcst")
    em.emit("gpsimd", lambda g: g.dma_start(out=bo_sb[:, :], in_=dr["bo_t"][:, :]), inc="dcst")
    em.emit("gpsimd", lambda g: g.dma_start(out=bv_sb[:, :], in_=dr["bv_t"][:, :]), inc="dcst")

    esK = ExitStack()
    kt_all = esK.enter_context(nc.sbuf_tensor("kt_all", [128, NKC * T], BF16))
    v_all = esK.enter_context(nc.sbuf_tensor("v_all", [128, NKCH * VST], BF16))
    qt_all = esK.enter_context(nc.sbuf_tensor("qt_all", [128, NQC * TOK], BF16))
    esA = ExitStack()
    sbA = lambda n, s, d: esA.enter_context(nc.sbuf_tensor(n, list(s), d))
    wk_sb = sbA("wk_sb", (128, 2 * NDC * 128), BF16)
    wv_sb = sbA("wv_sb", (128, NDC * TW), BF16)
    wq_sb = sbA("wq_sb", (128, 2 * NDC * 128), BF16)
    xt_sb = sbA("xt_sb", (128, 2 * NDC * TW), BF16)
    xf_sb = sbA("xf_sb", (128, 2 * TW), F32)
    cos_sb = sbA("cos_sb", (128, TW), F32)
    sin_sb = sbA("sin_sb", (128, TW), F32)
    scrf = sbA("scrf", (128, 2 * TW), F32)
    scrb = sbA("scrb", (128, 2 * TW), BF16)
    stat = sbA("stat", (1, 2 * TW), F32)

    for tc in range(NKCH):
        vv = v_all[:, VST * tc:VST * (tc + 1)].rearrange("p (g c) -> p g c", c=65)
        em.emit("vector", (lambda vv: lambda v: v.memset(vv[:, :, 64:65], 1.0))(vv))
    tok_wv = em.emit("gpsimd", lambda g: g.dma_start(out=wv_sb[:, :], in_=dr["wv_t"][:, :]), inc="dcst")
    tok_consts = tok_wv
    K["consts"] = tok_consts
    K["init_ve"] = em.tok(S_VE)

    pstat = pb01[:, 0:512]
    pnorm = pb01[:, 512:1024]
    pmub = pb01[:, 1024:1536]
    prb = pb01[:, 1536:2048]
    pp = [pb2[:, 0:512], pb2[:, 512:1024], pb3[:, 0:512], pb3[:, 512:1024]]

    state = dict(xt_free={}, rope_done=None, last_red=None,
                 wk_use={}, wq_use={}, pp_free=[None] * 4, scrb_war={},
                 prev_stats_consumed=None, nk_i=0, nq_i=0, xn_prev={})

    def A_tile(tt):
        own = tt < 2
        tpar = tt % 2
        xb = 8192 * tpar
        wx = [state["xt_free"].get(tpar), state.get("dbg_xn")]
        tok_xq = {}
        for qq in range(4):
            tok_xq[qq] = em.emit("gpsimd", (lambda tt, qq, xb: lambda g: g.dma_start(
                out=xt_sb[:, xb + 2048 * qq:xb + 2048 * (qq + 1)],
                in_=dr["x_t"][tt, :, 2048 * qq:2048 * (qq + 1)]))(tt, qq, xb),
                waits=(wx if qq == 0 else []), inc=f"dxt{tpar}{qq}")
        tok_x = tok_xq[3]
        tok_cos = em.emit("sync", (lambda tt: lambda s: s.dma_start(
            out=cos_sb[:, :], in_=dr["cos_t"][tt, :, :]))(tt),
            waits=[state["rope_done"]], inc="dcos")
        tok_sin = em.emit("sync", (lambda tt: lambda s: s.dma_start(
            out=sin_sb[:, :], in_=dr["sin_t"][tt, :, :]))(tt),
            waits=[state["rope_done"]], inc="dsin")

        for dc in range(NDC):
            xs = xt_sb[:, xb + TW * dc:xb + TW * (dc + 1)]
            w = [tok_xq[dc // 4]]
            if dc == 0:
                w += [state["prev_stats_consumed"], K["init_ve"]]
            em.emit("tensor", (lambda xs, dc: lambda t: t.matmul(
                pstat[0:1, :], ones_bf[:, 0:1], xs,
                start=(dc == 0), stop=(dc == NDC - 1)))(xs, dc), waits=w)
        prev_sq_mm = {}
        sqv = {}
        for dc in range(NDC):
            xs = xt_sb[:, xb + TW * dc:xb + TW * (dc + 1)]
            sp = dc % 2
            sqv[dc] = em.emit("scalar", (lambda xs, sp: lambda s: s.activation(
                scrb[:, TW * sp:TW * (sp + 1)], xs, AF.Square))(xs, sp),
                waits=[tok_xq[dc // 4], prev_sq_mm.get(dc - 2), state["scrb_war"].get(sp),
                       state.get("scrb_pe") if dc < 2 else None], inc=S_SC)
            prev_sq_mm[dc] = em.emit("tensor", (lambda dc, sp: lambda t: t.matmul(
                pstat[32:33, :], ones_bf[:, 0:1], scrb[:, TW * sp:TW * (sp + 1)],
                start=(dc == 0), stop=(dc == NDC - 1)))(dc, sp),
                waits=[sqv[dc]], inc=S_PE)
        stats_mm = prev_sq_mm[NDC - 1]

        mu_ap = muown[0:1, TW * tt:TW * (tt + 1)] if own else stat[0:1, 0:TW]
        wk_row = stat[0:1, TW:2 * TW]
        r_ap = rown[0:1, TW * tt:TW * (tt + 1)] if own else wk_row
        t1 = em.emit("vector", (lambda mu_ap: lambda v: v.tensor_scalar_mul(
            mu_ap, pstat[0:1, :], 1.0 / D))(mu_ap),
            waits=[stats_mm, state.get("prev_tmub")])
        t2 = em.emit("vector", (lambda mu_ap: lambda v: v.tensor_tensor(
            wk_row, mu_ap, mu_ap, ALU.mult))(mu_ap), waits=[state.get("prev_trb")])
        t3 = em.emit("vector", lambda v: v.scalar_tensor_tensor(
            wk_row, pstat[32:33, :], 1.0 / D, wk_row, ALU.mult, ALU.subtract))
        t4 = em.emit("vector", lambda v: v.tensor_scalar_add(wk_row, wk_row, EPS))
        t5 = em.emit("scalar", lambda s: s.activation(wk_row, wk_row, AF.Ln),
                     waits=[t4], inc=S_SC)
        tmub = em.emit("tensor", (lambda mu_ap: lambda t: t.matmul(
            pmub[:, :], ones_f[0:1, :], mu_ap, start=True, stop=True))(mu_ap),
            waits=[t1, em.tok(S_VE)])
        t6 = em.emit("scalar", (lambda r_ap: lambda s: s.activation(
            r_ap, wk_row, AF.Exp, scale=-0.5))(r_ap), waits=[t5], inc=S_SC)
        trb = em.emit("tensor", (lambda r_ap: lambda t: t.matmul(
            prb[:, :], ones_f[0:1, :], r_ap, start=True, stop=True))(r_ap),
            waits=[t6], inc=S_PE)
        state["prev_tmub"] = trb
        state["prev_trb"] = trb
        bcast = trb

        xn_tok = {}
        for dc in range(NDC):
            par = dc % 2
            xf = xf_sb[:, TW * par:TW * (par + 1)]
            w_xf = [xn_tok.get(dc - 2) or state["xn_prev"].get(NDC - 2 + dc)]
            tok_xf = em.emit("sync", (lambda tt, dc, xf: lambda s: s.dma_start(
                out=xf, in_=dr["x_t"][tt, :, TW * dc:TW * (dc + 1)]))(tt, dc, xf),
                waits=w_xf, inc=f"dxf{par}")
            ts_ = em.emit("vector", (lambda xf: lambda v: v.tensor_tensor(
                scrf[:, 0:TW], xf, pmub[:, :], ALU.subtract))(xf),
                waits=[tok_xf, bcast])
            xn_tok[dc] = em.emit("vector", (lambda dc, xb: lambda v: v.tensor_tensor(
                xt_sb[:, xb + TW * dc:xb + TW * (dc + 1)], scrf[:, 0:TW], prb[:, :],
                ALU.mult))(dc, xb), waits=[stats_mm])
        state["xn_prev"] = xn_tok
        state["prev_stats_consumed"] = t3
        if "d_xn" in dbg and tt == 0:
            state["dbg_xn"] = em.emit("gpsimd", lambda g: g.dma_start(
                out=dbg["d_xn"][:, :], in_=xt_sb[:, 0:NDC * TW]),
                waits=[xn_tok[NDC - 1]], inc="ddbg")

        def proj_oc(oc, wbuf, tok_w, b_sb, tgt_all, off, ncoll, nidx):
            par = oc % 2
            ppj = pp[par]
            prot = pp[2 + par]
            mtk = None
            for dc in range(NDC):
                w = [xn_tok[dc], tok_w]
                if dc == 0:
                    w += [state["pp_free"][par]]
                mtk = em.emit("tensor", (lambda ppj, wbuf, dc, xb: lambda t: t.matmul(
                    ppj[:, :], wbuf[:, 128 * dc:128 * (dc + 1)],
                    xt_sb[:, xb + TW * dc:xb + TW * (dc + 1)],
                    start=(dc == 0), stop=(dc == NDC - 1)))(ppj, wbuf, dc, xb),
                    waits=w, inc=(S_PE if dc == NDC - 1 else None))
            ev1 = em.emit("scalar", (lambda ppj, oc, b_sb: lambda s: s.activation(
                scrb[:, 0:TW], ppj[:, :], AF.Identity, bias=b_sb[:, oc:oc + 1]))(ppj, oc, b_sb),
                waits=[mtk, tok_consts, em.tok(S_VE), em.tok(S_SC)], inc=S_SC)
            state["scrb_war"][0] = ev1
            rtk = em.emit("tensor", (lambda prot, par: lambda t: t.matmul(
                prot[:, :], rot_sb[:, :], scrb[:, 0:TW], start=True, stop=True))(prot, par),
                waits=[ev1, tok_consts, state["pp_free"][2 + par]], inc=S_PE)
            r1 = em.emit("vector", lambda v: v.tensor_tensor(
                scrf[:, 0:TW], scrb[:, 0:TW], cos_sb[:, :], ALU.mult),
                waits=[tok_cos, ev1])
            r2 = em.emit("vector", (lambda prot: lambda v: v.tensor_tensor(
                scrf[:, TW:2 * TW], prot[:, :], sin_sb[:, :], ALU.mult))(prot),
                waits=[rtk, tok_sin])
            r3 = em.emit("vector", (lambda tgt_all, off: lambda v: v.tensor_tensor(
                tgt_all[:, off:off + TW], scrf[:, 0:TW], scrf[:, TW:2 * TW], ALU.add))(tgt_all, off))
            state["pp_free"][par] = r3
            state["pp_free"][2 + par] = r3
            sq = em.emit("vector", (lambda tgt_all, off: lambda v: v.tensor_tensor(
                scrb[:, 0:TW], tgt_all[:, off:off + TW], tgt_all[:, off:off + TW],
                ALU.mult))(tgt_all, off), waits=[rtk])
            state["scrb_war"][0] = sq
            nm = em.emit("tensor", lambda t: t.matmul(
                pnorm[0:1, :], ones_bf[:, 0:1], scrb[:, 0:TW], start=True, stop=True),
                waits=[sq, state["last_red"]], inc=S_PE)
            state["last_red"] = em.emit("vector", (lambda ncoll, nidx: lambda v: v.tensor_reduce(
                ncoll[0:1, nidx:nidx + 1], pnorm[0:1, :], AX.X, ALU.max))(ncoll, nidx),
                waits=[nm])
            return mtk

        for oc in range(NKC):
            wpar = (tt * NKC + oc) % 2
            wslot = wk_sb[:, 2048 * wpar:2048 * (wpar + 1)]
            tok_w = em.emit("gpsimd", (lambda oc, wslot: lambda g: g.dma_start(
                out=wslot, in_=dr["wk_t"][oc, :, :]))(oc, wslot),
                waits=[state["wk_use"].get(wpar)], inc=f"dwk{wpar}")
            state["wk_use"][wpar] = proj_oc(oc, wslot, tok_w, bk_sb, kt_all,
                                            4096 * oc + TW * tt, nk_sb, state["nk_i"])
            state["nk_i"] += 1

        for sub in range(4):
            par = sub % 2
            ppj = pp[par]
            mtk = None
            for dc in range(NDC):
                w = [xn_tok[dc], tok_wv]
                if dc == 0:
                    w += [state["pp_free"][par]]
                mtk = em.emit("tensor", (lambda ppj, dc, sub, xb: lambda t: t.matmul(
                    ppj[:, :], xt_sb[:, xb + TW * dc + 128 * sub: xb + TW * dc + 128 * sub + 128],
                    wv_sb[:, TW * dc:TW * (dc + 1)],
                    start=(dc == 0), stop=False))(ppj, dc, sub, xb), waits=w)
            mtk = em.emit("tensor", (lambda ppj: lambda t: t.matmul(
                ppj[:, :], ones_bf[0:1, :], bv_sb[0:1, :], start=False, stop=True))(ppj),
                waits=[tok_consts], inc=S_PE)
            tc = tt * 4 + sub
            ev = None
            for g in range(8):
                ev = em.emit("scalar", (lambda ppj, tc, g: lambda s: s.copy(
                    v_all[:, VST * tc + 65 * g: VST * tc + 65 * g + 64],
                    ppj[:, 64 * g:64 * g + 64]))(ppj, tc, g),
                    waits=[mtk] + ([em.tok(S_VE)] if g == 0 else []),
                    inc=(S_SC if g == 7 else None))
            state["pp_free"][par] = ev

        if own:
            for oc in range(NQC):
                wpar = oc % 2
                wslot = wq_sb[:, 2048 * wpar:2048 * (wpar + 1)]
                tok_w = em.emit("gpsimd", (lambda oc, wslot: lambda g: g.dma_start(
                    out=wslot, in_=dr["wq_t"][oc, :, :]))(oc, wslot),
                    waits=[state["wq_use"].get(wpar)], inc=f"dwq{wpar}")
                state["wq_use"][wpar] = proj_oc(oc, wslot, tok_w, bq_sb, qt_all,
                                                TOK * oc + TW * tt, nq_sb, state["nq_i"])
                state["nq_i"] += 1

        state["xt_free"][tpar] = em.tok(S_PE)
        state["scrb_pe"] = em.tok(S_PE)
        state["rope_done"] = em.tok(S_VE)

    for tt in range(NTT):
        A_tile(tt)

    a1 = em.emit("vector", lambda v: v.tensor_reduce(
        sc1_sb[0:1, 0:1], nq_sb[0:1, 0:32], AX.X, ALU.max), waits=[state["last_red"]])
    a2 = em.emit("vector", lambda v: v.tensor_reduce(
        sc1_sb[0:1, 1:2], nk_sb[0:1, 0:32], AX.X, ALU.max))
    a3 = em.emit("vector", lambda v: v.tensor_tensor(
        sc1_sb[0:1, 2:3], sc1_sb[0:1, 0:1], sc1_sb[0:1, 1:2], ALU.mult))
    a4l = em.emit("scalar", lambda s: s.activation(
        sc1_sb[0:1, 5:6], sc1_sb[0:1, 2:3], AF.Ln), waits=[a3], inc=S_SC)
    a4 = em.emit("scalar", lambda s: s.activation(
        sc1_sb[0:1, 3:4], sc1_sb[0:1, 5:6], AF.Exp, scale=0.5), waits=[a4l], inc=S_SC)
    a5 = em.emit("vector", lambda v: v.tensor_scalar_mul(
        sc1_sb[0:1, 4:5], sc1_sb[0:1, 3:4], -0.125), waits=[a4])
    a6 = em.emit("tensor", lambda t: t.matmul(
        pb01[:, 0:1], ones_f[0:1, :], sc1_sb[0:1, 4:5], start=True, stop=True),
        waits=[a5, em.tok(S_VE)], inc=S_PE)
    a7 = em.emit("vector", lambda v: v.tensor_copy(negb_sb[:, :], pb01[:, 0:1]),
                 waits=[a6])
    K["negb"] = a7
    for nm_, buf in (("d_negb", negb_sb), ("d_kt", kt_all), ("d_qt", qt_all), ("d_v", v_all)):
        if nm_ in dbg:
            em.emit("gpsimd", (lambda nm_, buf: lambda g: g.dma_start(
                out=dbg[nm_][:, :], in_=buf[:, :]))(nm_, buf),
                waits=[em.tok(S_VE), em.tok(S_SC)], inc="ddbg")
    esA.close()

    esB = ExitStack()
    attn_sb = es.enter_context(nc.sbuf_tensor("attn_sb", [128, NPAIR * TOK], BF16, side="right"))
    ptab = esB.enter_context(nc.sbuf_tensor("ptab", [128, 2 * 2 * TOK], BF16))
    rz_sb = esB.enter_context(nc.sbuf_tensor("rz_sb", [1, 2 * TOK], F32))
    rzs_sb = esB.enter_context(nc.sbuf_tensor("rzs_sb", [1, TOK], F32))
    atmp = esB.enter_context(nc.sbuf_tensor("atmp", [128, TOK], BF16))

    pscore = pb01
    pava, pavb = pb2, pb3
    negb_tok = K["negb"]
    a_ready = em.tok(S_VE)
    a_ready_sc = em.tok(S_SC)
    prev_evac = None
    av_final = None

    for j in range(NPAIR):
        p = j // 4
        m1 = em.emit("vector", lambda v: v.memset(pavb[0:1, :], 0.0),
                     waits=[av_final, prev_evac])
        em.emit("vector", lambda v: v.memset(pavb[64:128, :], 0.0))
        bmemset = em.emit("vector", lambda v: v.memset(pava[0:65, :], 0.0))
        exp_ab = {}
        av_a = {}
        av_b = {}
        s_ab = {}
        if j == 0:
            prev_pair_exp = {}

        def S_half(c, h):
            w_s = [a_ready, a_ready_sc, negb_tok, exp_ab.get((c - 1, h))]
            if j > 0 and c == 0:
                w_s.append(prev_pair_exp.get(h))
            last = None
            for hh in range(2):
                base = 64 * hh
                off = 1024 * h + 512 * hh
                last = em.emit("tensor", (lambda p, j, c, h, base, off: lambda t: t.matmul(
                    pscore[:, off:off + 512],
                    kt_all[base:base + 64, 4096 * p + 128 * c:4096 * p + 128 * (c + 1)],
                    qt_all[base:base + 64, TOK * j + 512 * h:TOK * j + 512 * (h + 1)],
                    start=True, stop=True))(p, j, c, h, base, off),
                    waits=(w_s if hh == 0 else []),
                    inc=(S_PE if hh == 1 else None))
            s_ab[(c, h)] = last

        def EXP(c, h):
            par = c % 2
            w_e = [s_ab[(c, h)], av_b.get((c - 2, h))]
            exp_ab[(c, h)] = em.emit("scalar", (lambda par, h: lambda s: s.activation(
                ptab[:, 2048 * par + 1024 * h:2048 * par + 1024 * (h + 1)],
                pscore[:, 1024 * h:1024 * (h + 1)], AF.Exp,
                bias=negb_sb[:, :], scale=0.125))(par, h), waits=w_e, inc=S_SC)

        def AV(c, h):
            par = c % 2
            first = (c == 0)
            pt_a = ptab[:, 2048 * par + 1024 * h:2048 * par + 1024 * h + 512]
            pt_b = ptab[:, 2048 * par + 1024 * h + 512:2048 * par + 1024 * (h + 1)]
            w_av = [exp_ab[(c, h)]]
            if first and h == 0:
                w_av += [prev_evac, bmemset]
            em.emit("tensor", (lambda c, p, h, pt_a: lambda t: t.matmul(
                pava[0:64, 512 * h:512 * (h + 1)],
                v_all[:, VST * c + 65 * (2 * p):VST * c + 65 * (2 * p) + 64],
                pt_a, start=False, stop=False, skip_group_check=True,
                tile_position=(0, 0)))(c, p, h, pt_a), waits=w_av)
            em.emit("tensor", (lambda c, p, h, pt_b: lambda t: t.matmul(
                pavb[64:128, 512 * h:512 * (h + 1)],
                v_all[:, VST * c + 65 * (2 * p + 1):VST * c + 65 * (2 * p + 1) + 64],
                pt_b, start=False, stop=False, skip_group_check=True,
                tile_position=(0, 64)))(c, p, h, pt_b))
            em.emit("tensor", (lambda c, h, pt_a: lambda t: t.matmul(
                pava[64:65, 512 * h:512 * (h + 1)], ones_bf[:, 0:1],
                pt_a, start=False, stop=False, skip_group_check=True,
                tile_position=(0, 64)))(c, h, pt_a))
            av_b[(c, h)] = em.emit("tensor", (lambda c, h, pt_b: lambda t: t.matmul(
                pavb[0:1, 512 * h:512 * (h + 1)], ones_bf[:, 0:1],
                pt_b, start=False, stop=False, skip_group_check=True,
                tile_position=(0, 0)))(c, h, pt_b), inc=S_PE)

        for c in range(NKCH):
            for h in range(2):
                S_half(c, h)
                EXP(c, h)
                if c >= 1:
                    AV(c - 1, h)
        AV(NKCH - 1, 0)
        AV(NKCH - 1, 1)
        av_final = av_b[(NKCH - 1, 1)]
        prev_pair_exp = {h: exp_ab[(NKCH - 1, h)] for h in range(2)}
        q1 = em.emit("vector", lambda v: v.reciprocal(rz_sb[0:1, 0:TOK], pava[64:65, :]),
                     waits=[av_final])
        q2 = em.emit("vector", lambda v: v.reciprocal(rz_sb[0:1, TOK:2 * TOK], pavb[0:1, :]))
        q3 = em.emit("vector", lambda v: v.tensor_copy(atmp[0:64, :], pava[0:64, :]),
                     waits=[av_final])
        q4 = em.emit("vector", lambda v: v.tensor_copy(atmp[64:128, :], pavb[64:128, :]))
        bt = None
        for h in range(2):
            em.emit("tensor", (lambda h: lambda t: t.matmul(
                pava[0:64, 512 * h:512 * (h + 1)], ones_f[0:1, 0:64],
                rz_sb[0:1, 512 * h:512 * (h + 1)], start=True, stop=True,
                skip_group_check=True))(h), waits=[q2, q3])
            bt = em.emit("tensor", (lambda h: lambda t: t.matmul(
                pavb[64:128, 512 * h:512 * (h + 1)], ones_f[0:1, 0:64],
                rz_sb[0:1, TOK + 512 * h:TOK + 512 * (h + 1)], start=True, stop=True,
                skip_group_check=True))(h),
                waits=[q4], inc=(S_PE if h == 1 else None))
        em.emit("vector", (lambda j: lambda v: v.tensor_tensor(
            attn_sb[0:64, TOK * j:TOK * (j + 1)], atmp[0:64, :], pava[0:64, :],
            ALU.mult))(j), waits=[bt])
        prev_evac = em.emit("vector", (lambda j: lambda v: v.tensor_tensor(
            attn_sb[64:128, TOK * j:TOK * (j + 1)], atmp[64:128, :], pavb[64:128, :],
            ALU.mult))(j))
    K["attn_done"] = prev_evac
    if "d_attn" in dbg:
        em.emit("gpsimd", lambda g: g.dma_start(out=dbg["d_attn"][:, :], in_=attn_sb[:, :]),
                waits=[prev_evac], inc="ddbg")
    esB.close()
    esK.close()

    esC = ExitStack()
    sbC = lambda n, s, d: esC.enter_context(nc.sbuf_tensor(n, list(s), d))
    gated = sbC("gated", (128, NHC * TW), BF16)
    xt2 = sbC("xt2", (128, NDC * TW), BF16)
    xf2 = sbC("xf2", (128, 2 * TW), F32)
    scr2 = sbC("scr2", (128, TW), F32)
    silu_sb = sbC("silu_sb", (128, 2 * TW), BF16)
    wg_sb = sbC("wg_sb", (128, 2 * NDC * 128), BF16)
    wu_sb = sbC("wu_sb", (128, 2 * NDC * 128), BF16)
    wd_sb = sbC("wd_sb", (128, 2 * NHC * 128), BF16)
    wo_sb = sbC("wo_sb", (128, 2 * NQC * 128), BF16)
    xr_sb = sbC("xr_sb", (128, 2 * TW), F32)
    out_sb = sbC("out_sb", (128, 2 * TW), F32)

    pmub2 = pb01[:, 0:512]
    prb2 = pb01[:, 512:1024]
    pg = [pb01[:, 1024:1536], pb01[:, 1536:2048]]
    pu = [pb2[:, 0:512], pb2[:, 512:1024]]
    pres = [pb3[:, 0:512], pb3[:, 512:1024]]
    attn_done = K["attn_done"]
    dbg_tok = em.tok("ddbg") if dbg else None
    cd = dict(wg_last={}, wu_last={}, wd_last={}, wo_last={}, gate_free={},
              silu_free={}, ev_last={}, st_last={}, xr_last={}, n_out=0,
              xn2_carry=None)

    for th in range(2):
        w0 = ([attn_done, dbg_tok] if th == 0
              else [em.tok(S_PE), em.tok(S_VE), em.tok("ddbg") if dbg else None])
        tok_x2 = em.emit("gpsimd", (lambda th: lambda g: g.dma_start(
            out=xt2[:, :], in_=dr["x_t"][th, :, :]))(th), waits=w0, inc="dxt2")
        bc1 = em.emit("tensor", (lambda th: lambda t: t.matmul(
            pmub2[:, :], ones_f[0:1, :], muown[0:1, TW * th:TW * (th + 1)],
            start=True, stop=True))(th), waits=[attn_done] + (w0 if th else []))
        bc2 = em.emit("tensor", (lambda th: lambda t: t.matmul(
            prb2[:, :], ones_f[0:1, :], rown[0:1, TW * th:TW * (th + 1)],
            start=True, stop=True))(th), inc=S_PE)
        xn2_tok = {}
        for dc in range(NDC):
            par = dc % 2
            xf = xf2[:, TW * par:TW * (par + 1)]
            w = [xn2_tok.get(dc - 2) or cd["xn2_carry"] or attn_done, dbg_tok]
            tok_xf = em.emit("sync", (lambda th, dc, xf: lambda s: s.dma_start(
                out=xf, in_=dr["x_t"][th, :, TW * dc:TW * (dc + 1)]))(th, dc, xf),
                waits=w, inc=f"dxf2{par}")
            em.emit("vector", (lambda xf: lambda v: v.tensor_tensor(
                scr2[:, :], xf, pmub2[:, :], ALU.subtract))(xf),
                waits=[tok_xf, bc2])
            xn2_tok[dc] = em.emit("vector", (lambda dc: lambda v: v.tensor_tensor(
                xt2[:, TW * dc:TW * (dc + 1)], scr2[:, :], prb2[:, :], ALU.mult))(dc),
                waits=[tok_x2])
        cd["xn2_carry"] = xn2_tok[NDC - 1]

        gate_ev = {}
        for hc in range(NHC):
            par = hc % 2
            tok_wg = em.emit("gpsimd", (lambda hc, par: lambda g: g.dma_start(
                out=wg_sb[:, 2048 * par:2048 * (par + 1)], in_=dr["wg_t"][hc, :, :]))(hc, par),
                waits=[cd["wg_last"].get(par)], inc=f"dwg{par}")
            tok_wu = em.emit("gpsimd", (lambda hc, par: lambda g: g.dma_start(
                out=wu_sb[:, 2048 * par:2048 * (par + 1)], in_=dr["wu_t"][hc, :, :]))(hc, par),
                waits=[cd["wu_last"].get(par)], inc=f"dwu{par}")
            mg = mu_ = None
            for dc in range(NDC):
                w = [xn2_tok[dc], tok_wg]
                if dc == 0:
                    w.append(cd["silu_free"].get(par))
                mg = em.emit("tensor", (lambda par, dc: lambda t: t.matmul(
                    pg[par][:, :], wg_sb[:, 2048 * par + 128 * dc:2048 * par + 128 * (dc + 1)],
                    xt2[:, TW * dc:TW * (dc + 1)],
                    start=(dc == 0), stop=(dc == NDC - 1)))(par, dc),
                    waits=w, inc=(S_PE if dc == NDC - 1 else None))
            cd["wg_last"][par] = mg
            for dc in range(NDC):
                w = [xn2_tok[dc], tok_wu]
                if dc == 0:
                    w.append(cd["gate_free"].get(par))
                mu_ = em.emit("tensor", (lambda par, dc: lambda t: t.matmul(
                    pu[par][:, :], wu_sb[:, 2048 * par + 128 * dc:2048 * par + 128 * (dc + 1)],
                    xt2[:, TW * dc:TW * (dc + 1)],
                    start=(dc == 0), stop=(dc == NDC - 1)))(par, dc),
                    waits=w, inc=(S_PE if dc == NDC - 1 else None))
            cd["wu_last"][par] = mu_
            sl = em.emit("scalar", (lambda par: lambda s: s.activation(
                silu_sb[:, TW * par:TW * (par + 1)], pg[par][:, :], AF.Silu))(par),
                waits=[mg, cd["gate_free"].get(par)], inc=S_SC)
            gate_ev[hc] = em.emit("vector", (lambda hc, par: lambda v: v.tensor_tensor(
                gated[:, TW * hc:TW * (hc + 1)], silu_sb[:, TW * par:TW * (par + 1)],
                pu[par][:, :], ALU.mult))(hc, par), waits=[sl, mu_])
            cd["gate_free"][par] = gate_ev[hc]
            cd["silu_free"][par] = gate_ev[hc]

        for oc in range(NQC):
            par = oc % 2
            tok_wd = em.emit("gpsimd", (lambda oc, par: lambda g: g.dma_start(
                out=wd_sb[:, 8192 * par:8192 * (par + 1)], in_=dr["wd_t"][oc, :, :]))(oc, par),
                waits=[cd["wd_last"].get(par)], inc=f"dwd{par}")
            tok_wo = em.emit("gpsimd", (lambda oc, par: lambda g: g.dma_start(
                out=wo_sb[:, 2048 * par:2048 * (par + 1)], in_=dr["wo_t"][oc, :, :]))(oc, par),
                waits=[cd["wo_last"].get(par)], inc=f"dwo{par}")
            tok_xr = em.emit("sync", (lambda oc, par, th: lambda s: s.dma_start(
                out=xr_sb[:, TW * par:TW * (par + 1)],
                in_=dr["xr_t"][oc, :, TW * th:TW * (th + 1)]))(oc, par, th),
                waits=[cd["xr_last"].get(par)], inc=f"dxr{par}")
            mm = None
            for cc in range(NQC):
                w = [attn_done, tok_wo]
                if cc == 0:
                    w.append(cd["ev_last"].get(par))
                mm = em.emit("tensor", (lambda par, cc, th: lambda t: t.matmul(
                    pres[par][:, :], wo_sb[:, 2048 * par + 128 * cc:2048 * par + 128 * (cc + 1)],
                    attn_sb[:, TOK * cc + TW * th:TOK * cc + TW * (th + 1)],
                    start=(cc == 0), stop=False))(par, cc, th), waits=w)
            for hc in range(NHC):
                mm = em.emit("tensor", (lambda par, hc: lambda t: t.matmul(
                    pres[par][:, :], wd_sb[:, 8192 * par + 128 * hc:8192 * par + 128 * (hc + 1)],
                    gated[:, TW * hc:TW * (hc + 1)],
                    start=False, stop=(hc == NHC - 1)))(par, hc),
                    waits=[gate_ev[hc], tok_wd], inc=(S_PE if hc == NHC - 1 else None))
            cd["wd_last"][par] = mm
            cd["wo_last"][par] = mm
            ev = em.emit("vector", (lambda par, oc: lambda v: v.scalar_tensor_tensor(
                out_sb[:, TW * par:TW * (par + 1)], pres[par][:, :], bo_sb[:, oc:oc + 1],
                xr_sb[:, TW * par:TW * (par + 1)], ALU.add, ALU.add))(par, oc),
                waits=[mm, tok_xr, cd["st_last"].get(par), K["consts"]])
            cd["ev_last"][par] = ev
            cd["xr_last"][par] = ev
            st = em.emit("sync", (lambda par, oc, th: lambda s: s.dma_start(
                out=out_t[oc, :, TW * th:TW * (th + 1)],
                in_=out_sb[:, TW * par:TW * (par + 1)]))(par, oc, th),
                waits=[ev], inc=f"dout{par}")
            cd["st_last"][par] = st
            cd["n_out"] += 1
        if "d_gated" in dbg and th == 0:
            em.emit("gpsimd", lambda g: g.dma_start(out=dbg["d_gated"][:, :], in_=gated[:, :]),
                    waits=[gate_ev[NHC - 1]], inc="ddbg")

    final_waits = [em.tok("dout0"), em.tok("dout1")]
    if dbg:
        final_waits.append(em.tok("ddbg"))
    em.emit("sync", lambda s: None, waits=final_waits)
    esC.close()

    em.check_deadlock()
    sem_es = ExitStack()
    sems = {n: sem_es.enter_context(nc.semaphore(n)) for n in sorted(em.cnt)}
    blk = sem_es.enter_context(nc.Block())

    def emit_engine(eng_name):
        ops = em.ops[eng_name]
        def body(eng):
            waited = {}
            for fn, waits, inc in ops:
                for (s, v) in waits:
                    v16 = 16 * v if is_dma_sem(s) else v
                    if waited.get(s, -1) >= v16:
                        continue
                    eng.wait_ge(sems[s], v16)
                    waited[s] = v16
                inst = fn(eng)
                if inc is not None:
                    if inst is None:
                        raise RuntimeError(f"inc op returned None on {eng_name}")
                    inst.then_inc(sems[inc], 16 if is_dma_sem(inc) else 1)
        return body

    blk.tensor(emit_engine("tensor"))
    blk.vector(emit_engine("vector"))
    blk.scalar(emit_engine("scalar"))
    blk.sync(emit_engine("sync"))
    blk.gpsimd(emit_engine("gpsimd"))
    sem_es.close()
    es.close()
    _NC_CACHE[key] = nc
    return nc


def _prep_inputs(inp):
    f32 = np.float32
    x = np.ascontiguousarray(np.asarray(inp["L0_attn_x"], f32))
    perm = _q_perm()
    Wq = np.ascontiguousarray(np.asarray(inp["L0_attn_Wq"], f32)[perm, :])
    bq = np.ascontiguousarray(np.asarray(inp["L0_attn_bq"], f32)[perm])
    Wk = np.asarray(inp["L0_attn_Wk"], f32)
    bk = np.asarray(inp["L0_attn_bk"], f32)
    Wv = np.asarray(inp["L0_attn_Wv"], f32)
    bv = np.asarray(inp["L0_attn_bv"], f32)
    Wo = np.ascontiguousarray(np.asarray(inp["L0_attn_Wo"], f32)[:, perm])
    bo = np.asarray(inp["L0_attn_bo"], f32)
    Wg = np.asarray(inp["L0_ffn_w_gate"], f32)
    Wu = np.asarray(inp["L0_ffn_w_up"], f32)
    Wd = np.asarray(inp["L0_ffn_w_down"], f32)

    def blockT(W, noc):
        O, Kd = W.shape
        nkc = Kd // 128
        a = W.reshape(noc, 128, nkc, 128).transpose(0, 3, 2, 1)
        return np.ascontiguousarray(a.reshape(noc, 128, nkc * 128))

    wq_t = blockT(Wq, NQC)
    wk_t = blockT(Wk, NKC)
    wg_t = blockT(Wg, NHC)
    wu_t = blockT(Wu, NHC)
    wd_t = blockT(Wd, NQC)
    wo_t = blockT(Wo, NQC)
    wv_t = np.ascontiguousarray(Wv.reshape(512, NDC, 128).transpose(2, 1, 0).reshape(128, NDC * 512))

    rot = np.zeros((128, 128), f32)
    for m in range(128):
        rot[64 * (m // 64) + ((m % 64) + 32) % 64, m] = 1.0
    inv = 1.0 / (10000.0 ** (np.arange(0, HD, 2, dtype=np.float64) / HD))
    ang = np.arange(T, dtype=np.float64)[:, None] * inv[None, :]
    cosf = np.cos(ang).astype(f32)
    sinf = np.sin(ang).astype(f32)
    prow = np.arange(128)
    irow = (prow % 64) % 32
    sgn = np.where((prow % 64) < 32, -1.0, 1.0).astype(f32)
    cos_full = np.ascontiguousarray(cosf[:, irow].T)
    sin_full = np.ascontiguousarray(sinf[:, irow].T * sgn[:, None])

    bq_t = np.ascontiguousarray(bq.reshape(NQC, 128).T)
    bk_t = np.ascontiguousarray(bk.reshape(NKC, 128).T)
    bo_t = np.ascontiguousarray(bo.reshape(NQC, 128).T)
    bv_t = np.ascontiguousarray(bv.reshape(1, 512))

    shared = dict(wq_t=wq_t, wk_t=wk_t, wv_t=wv_t, wg_t=wg_t, wu_t=wu_t,
                  wd_t=wd_t, wo_t=wo_t, rot_t=rot, bq_t=bq_t, bk_t=bk_t,
                  bo_t=bo_t, bv_t=bv_t)

    in_maps = []
    for c in range(NCORES):
        b, s = c // 4, c % 4
        order = [2 * s, 2 * s + 1] + [i for i in range(NTT) if i not in (2 * s, 2 * s + 1)]
        xT = x[b].T
        xr4 = xT.reshape(NDC, 128, NTT, TW)
        x_t = np.ascontiguousarray(
            xr4.transpose(2, 1, 0, 3)[order].reshape(NTT, 128, NDC * TW))
        cos_t = np.ascontiguousarray(cos_full.reshape(128, NTT, TW).transpose(1, 0, 2)[order])
        sin_t = np.ascontiguousarray(sin_full.reshape(128, NTT, TW).transpose(1, 0, 2)[order])
        xr_t = np.ascontiguousarray(xT[:, TOK * s:TOK * (s + 1)].reshape(NDC, 128, TOK))
        m = dict(shared)
        m.update(x_t=x_t, cos_t=cos_t, sin_t=sin_t, xr_t=xr_t)
        in_maps.append(m)
    return in_maps


def kernel(**inputs):
    in_maps = _prep_inputs(inputs)
    nc = build_nc()
    res = run_bass_kernel_spmd(nc, in_maps, core_ids=list(range(NCORES)))
    out = np.empty((B, T, D), np.float32)
    for c in range(NCORES):
        b, s = c // 4, c % 4
        o = np.asarray(res.results[c]["out_t"])
        out[b, TOK * s:TOK * (s + 1), :] = o.transpose(2, 0, 1).reshape(TOK, D)
    return out
